# version 12
# baseline (speedup 1.0000x reference)
import math
from contextlib import ExitStack

import numpy as np
import concourse.bass as bass
import concourse.mybir as mybir
from concourse.bass_utils import run_bass_kernel_spmd

F32 = mybir.dt.float32
F16 = mybir.dt.float16
I32 = mybir.dt.int32
AF = mybir.ActivationFunctionType
ALU = mybir.AluOpType
AX = mybir.AxisListType

B, S, C, H, HD = 16, 2048, 1024, 16, 64
BPC = B // 8
NJT = S // 128
DVT = 35
ACT_T = 64 - DVT


def _mix_seq():
    seq = []
    d = a = 0
    while d < DVT or a < ACT_T:
        if a >= ACT_T or (d < DVT and d * ACT_T <= a * DVT):
            seq.append(("d", d)); d += 1
        else:
            seq.append(("a", a)); a += 1
    return seq


MIX = _mix_seq()
POS_D = {i: p for p, (s, i) in enumerate(MIX) if s == "d"}
POS_A = {i: p for p, (s, i) in enumerate(MIX) if s == "a"}
B_EXP = 4.5
EPS_ATT = float(math.exp(B_EXP) * 1e-6)
I_FFN = 4096
DEBUG = False
RACES = False
TAP = "out"
STOP = False
ATTONLY = False


def build_nc():
    nc = bass.Bass("TRN2", target_bir_lowering=False, detect_race_conditions=RACES)

    x_in = nc.dram_tensor("x", [BPC, S, C], F32, kind="ExternalInput").ap()
    qkvw = nc.dram_tensor("qkv_w", [3 * C, C], F32, kind="ExternalInput").ap()
    qkvb = nc.dram_tensor("qkv_b", [3 * C], F32, kind="ExternalInput").ap()
    projw = nc.dram_tensor("proj_w", [C, C], F32, kind="ExternalInput").ap()
    projb_h = nc.dram_tensor("proj_b", [C], F32)
    projb = projb_h.ap()
    dwcw = nc.dram_tensor("dwc_w", [C, 3], F32, kind="ExternalInput").ap()
    dwcb = nc.dram_tensor("dwc_b", [C], F32, kind="ExternalInput").ap()
    wg_d = nc.dram_tensor("w_gate", [I_FFN, C], F32, kind="ExternalInput").ap()
    wu_d = nc.dram_tensor("w_up", [I_FFN, C], F32, kind="ExternalInput").ap()
    wd_d = nc.dram_tensor("w_down", [C, I_FFN], F32, kind="ExternalInput").ap()
    id_in = nc.dram_tensor("identity16", [128, 128], F16, kind="ExternalInput").ap()
    mk_in = nc.dram_tensor("masks16", [128, 128 * 64], F16, kind="ExternalInput").ap()
    qkb_in = nc.dram_tensor("qkb_pre", [128, 24], F32, kind="ExternalInput").ap()
    dwb_in = nc.dram_tensor("dwb_pre", [128, 8], F32, kind="ExternalInput").ap()
    dww_in = nc.dram_tensor("dww_pre", [128, 24], F32, kind="ExternalInput").ap()
    pjb_in = nc.dram_tensor("prjb_pre", [128, 1024], F32, kind="ExternalInput").ap()
    out_d = nc.dram_tensor("out", [BPC, S, C], F32, kind="ExternalOutput").ap()

    qkvwT_s = nc.dram_tensor("qkvwT_s", [8, 24, 128, 128], F16).ap()
    projwT_s = nc.dram_tensor("projwT_s", [8, 2, 128, 512], F16).ap()
    wgT_s = nc.dram_tensor("wgT_s", [8, 8, 128, 512], F16).ap()
    wuT_s = nc.dram_tensor("wuT_s", [8, 8, 128, 512], F16).ap()
    wdT_s = nc.dram_tensor("wdT_s", [32, 2, 128, 512], F16).ap()
    h_s = nc.dram_tensor("h_s", [4, 8, 128, 512], F32).ap()
    sc_h = nc.dram_tensor("sc_s", [256], F32)
    sc_s = sc_h.ap()
    if DEBUG:
        dbg = {}
        dbg["kern"] = nc.dram_tensor("dbg_kern", [128, 2048], F16, kind="ExternalOutput").ap()
        dbg["vt1"] = nc.dram_tensor("dbg_vt1", [128, 2112], F16, kind="ExternalOutput").ap()
        dbg["avps"] = nc.dram_tensor("dbg_avps", [128, 1024], F32, kind="ExternalOutput").ap()
        dbg["gt"] = nc.dram_tensor("dbg_gt", [128, 1024], F16, kind="ExternalOutput").ap()
        dbg["gtb"] = nc.dram_tensor("dbg_gtb", [128, 1024], F16, kind="ExternalOutput").ap()
        dbg["prj0"] = nc.dram_tensor("dbg_prj0", [128, 1024], F32, kind="ExternalOutput").ap()
        dbg["xpb0"] = nc.dram_tensor("dbg_xpb0", [128, 1024], F32, kind="ExternalOutput").ap()
        dbg["cen0"] = nc.dram_tensor("dbg_cen0", [128, 1024], F32, kind="ExternalOutput").ap()
        dbg["x1q"] = nc.dram_tensor("dbg_x1q", [128, 4096], F16, kind="ExternalOutput").ap()
        dbg["hhb"] = nc.dram_tensor("dbg_hhb", [128, 512], F32, kind="ExternalOutput").ap()
        dbg["dne"] = nc.dram_tensor("dbg_dne", [128, 1024], F32, kind="ExternalOutput").ap()
        dbg["eg"] = nc.dram_tensor("dbg_eg", [128, 512], F32, kind="ExternalOutput").ap()
        dbg["ue"] = nc.dram_tensor("dbg_ue", [128, 512], F32, kind="ExternalOutput").ap()
        dbg["wgb"] = nc.dram_tensor("dbg_wgb", [128, 4096], F16, kind="ExternalOutput").ap()
        dbg["wgall"] = nc.dram_tensor("dbg_wgall", [8, 8, 128, 512], F16, kind="ExternalOutput").ap()
        dbg["wuall"] = nc.dram_tensor("dbg_wuall", [8, 8, 128, 512], F16, kind="ExternalOutput").ap()
        dbg["wdall"] = nc.dram_tensor("dbg_wdall", [32, 2, 128, 512], F16, kind="ExternalOutput").ap()
        dbg["scl"] = nc.dram_tensor("dbg_scl", [128, 8], F32, kind="ExternalOutput").ap()
        dbg["yp"] = nc.dram_tensor("dbg_yp", [128, 16384], F16, kind="ExternalOutput").ap()
    st = ExitStack()
    E = st.enter_context
    with st:
        ID16 = E(nc.sbuf_tensor("ID16", [128, 128], F16))
        ZRO16 = E(nc.sbuf_tensor("ZRO16", [128, 128], F16))
        MASKS = E(nc.sbuf_tensor("MASKS", [128, 128 * 64], F16))
        M7 = E(nc.sbuf_tensor("M7", [128, 1], I32))
        ONE32 = E(nc.sbuf_tensor("ONE32", [128, 1], F32))
        ONER = E(nc.sbuf_tensor("ONER", [1, 128], F32))
        CST = E(nc.sbuf_tensor("CST", [128, 8], F32))
        QKB = E(nc.sbuf_tensor("QKB", [128, 24], F32))
        QKBN = E(nc.sbuf_tensor("QKBN", [128, 24], F32))
        DWW = E(nc.sbuf_tensor("DWW", [128, 24], F32))
        DWB = E(nc.sbuf_tensor("DWB", [128, 8], F32))
        PRJB = E(nc.sbuf_tensor("PRJB", [128, 1024], F32))
        PWT = E(nc.sbuf_tensor("PWT", [128, 8192], F16))
        WSC = E(nc.sbuf_tensor("WSC", [128, 4], F32))
        SCL = E(nc.sbuf_tensor("SCL", [128, 8], F32))

        SEM = {k: E(nc.semaphore("sem_" + k)) for k in
               ("sy", "gp", "dv", "ac", "pe", "vq")}
        CNT = {k: 0 for k in SEM}
        LASTW = {}
        block = E(nc.Block())

        def inc(key, instr, amt=1):
            CNT[key] += amt
            instr.then_inc(SEM[key], amt)
            return CNT[key]

        def wait(eng, key, val):
            if val is None or val <= 0:
                return
            wk = (id(eng), key)
            if LASTW.get(wk, -1) >= val:
                return
            eng.wait_ge(SEM[key], val)
            LASTW[wk] = val

        @block.sync
        def _(sync):
            sync.dma_start(QKB[:], qkb_in).then_inc(SEM["sy"], 16)
            sync.dma_start(DWB[:], dwb_in).then_inc(SEM["sy"], 16)
            sync.dma_start(DWW[:], dww_in).then_inc(SEM["sy"], 16)
            sync.dma_start(ID16[:], id_in).then_inc(SEM["sy"], 16)
            sync.dma_start(MASKS[:], mk_in).then_inc(SEM["sy"], 16)
            sync.dma_start(PRJB[:], pjb_in).then_inc(SEM["sy"], 16)
            CNT["sy"] += 96
        ck_smallin = CNT["sy"]

        @block.vector
        def _(vector):
            vector.memset(M7[:], 0x7FFF7FFF)
            vector.memset(ONE32[:], 1.0)
            vector.memset(ONER[:], 1.0)
            for col, val in ((0, 1e-5), (1, 1.0 / 127.0), (2, 1.0 / 1024.0),
                             (3, 1.0 / (I_FFN * C)), (4, B_EXP), (5, 0.0),
                             (6, EPS_ATT)):
                vector.memset(CST[:, col:col + 1], val)
            vector.memset(ZRO16[:], 0.0)
            wait(vector, "sy", ck_smallin)
            inc("dv", vector.tensor_scalar(QKBN[:], QKB[:], -1.0, None, ALU.mult))
        ck_const = CNT["dv"]

        stw = ExitStack()
        Ew = stw.enter_context
        with stw:
            WLD = Ew(nc.sbuf_tensor("WLD", [128, 4096], F16))
            WEV = Ew(nc.sbuf_tensor("WEV", [128, 4096], F16))
            WF32 = Ew(nc.sbuf_tensor("WF32", [128, 4096], F32))
            WI32 = Ew(nc.sbuf_tensor("WI32", [128, 4096], I32))
            WQ32 = Ew(nc.sbuf_tensor("WQ32", [128, 4096], F32))
            ACC = Ew(nc.sbuf_tensor("ACC", [128, 4], F32))
            ACCR = Ew(nc.sbuf_tensor("ACCR", [1, 128], F32))
            PST = Ew(nc.psum_tensor("PST", [128, 2048], F32))
            PSR = Ew(nc.psum_tensor("PSR", [128, 2], F32))
            PSTH = PST.ap().bitcast(F16)
            last_store = [0]
            wld_free = [0]
            wq32_free = [0]
            wf32_free = [0]

            ck_prjb = CNT["ac"]

            def _trans_tile(fi, n_cch, dst_fn):
                @block.tensor
                def _(tensor):
                    wait(tensor, "ac", ck_prjb)
                    wait(tensor, "vq", last_store[0])
                    for ci in range(n_cch):
                        inc("pe", tensor.transpose(PSTH[:, 128 * ci:128 * (ci + 1)],
                                                   WLD[:, 128 * ci:128 * (ci + 1)], ID16[:]))
                wld_free[0] = CNT["pe"]
                ckt = CNT["pe"]
                @block.scalar
                def _(scalar):
                    wait(scalar, "pe", ckt)
                    inc("ac", scalar.copy(WEV[:, 0:128 * n_cch], PSTH[:, 0:128 * n_cch]))
                cke = CNT["ac"]
                @block.sync
                def _(sync):
                    wait(sync, "ac", cke)
                    for ci in range(n_cch):
                        sync.dma_start(dst_fn(fi, ci),
                                       WEV[:, 128 * ci:128 * (ci + 1)]).then_inc(SEM["vq"], 16)
                        CNT["vq"] += 16
                last_store[0] = CNT["vq"]

            def trans_store(w_ap, n_fch, csz, dst_fn):
                n_cch = csz // 128
                for fi in range(n_fch):
                    g0 = CNT["sy"]
                    wfree = wf32_free[0]
                    @block.sync
                    def _(sync):
                        wait(sync, "dv", wfree)
                        sync.dma_start(WF32[:, 0:csz],
                                       w_ap[128 * fi:128 * (fi + 1), :]).then_inc(SEM["sy"], 16)
                        CNT["sy"] += 16
                    wfree2 = wld_free[0]
                    @block.vector
                    def _(vector):
                        wait(vector, "sy", g0 + 16)
                        wait(vector, "pe", wfree2)
                        inc("dv", vector.tensor_copy(WLD[:, 0:csz], WF32[:, 0:csz]))
                    wf32_free[0] = CNT["dv"]
                    ckcv = CNT["dv"]
                    @block.tensor
                    def _(tensor):
                        wait(tensor, "dv", ckcv)
                    _trans_tile(fi, n_cch, dst_fn)

            trans_store(qkvw, 24, C, lambda fi, ci: qkvwT_s[ci, fi, :, :])
            trans_store(projw, 8, C,
                        lambda fi, ci: projwT_s[ci, fi // 4, :, 128 * (fi % 4):128 * (fi % 4) + 128])

            def ffn_quant(w_ap, scol, mcol, n_fch, csz, dst_fn):
                n_cch = csz // 128
                @block.vector
                def _(vector):
                    inc("dv", vector.memset(ACC[:, 0:1], 0.0))
                for fi in range(n_fch):
                    g0 = CNT["sy"]
                    wfree = wf32_free[0]
                    @block.sync
                    def _(sync):
                        wait(sync, "dv", wfree)
                        sync.dma_start(WF32[:, 0:csz],
                                       w_ap[128 * fi:128 * (fi + 1), :]).then_inc(SEM["sy"], 16)
                        CNT["sy"] += 16
                    @block.vector
                    def _(vector):
                        wait(vector, "sy", g0 + 16)
                        vector.tensor_reduce(ACC[:, 1:2], WF32[:, 0:csz], AX.X, ALU.add,
                                             apply_absolute_value=True)
                        vector.tensor_copy(ACC[:, 2:3], ACC[:, 1:2])
                        vector.tensor_copy(ACC[:, 2:3], ACC[:, 1:2])
                        vector.tensor_tensor(ACC[:, 0:1], ACC[:, 0:1],
                                             ACC[:, 1:2], ALU.add)
                        vector.tensor_copy(ACC[:, 2:3], ACC[:, 0:1])
                        inc("dv", vector.tensor_copy(ACC[:, 2:3], ACC[:, 0:1]))
                    wf32_free[0] = CNT["dv"]
                ckred = CNT["dv"]
                g1 = CNT["sy"]
                @block.sync
                def _(sync):
                    wait(sync, "dv", ckred)
                    sync.dma_start(sc_s.rearrange("(p one) -> p one", one=1)[0:128, :], ACC[:, 0:1]).then_inc(SEM["sy"], 16)
                    CNT["sy"] += 16
                    sync.wait_ge(SEM["sy"], CNT["sy"])
                    sync.dma_start(ACCR[0:1, :], sc_s.rearrange("(one k) -> one k", one=1)[0:1, 0:128]).then_inc(SEM["sy"], 16)
                    CNT["sy"] += 16
                @block.vector
                def _(vector):
                    wait(vector, "sy", g1 + 32)
                    vector.tensor_reduce(ACC[0:1, 1:2], ACCR[0:1, :], AX.X, ALU.add)
                    vector.tensor_copy(ACC[0:1, 0:1], ACC[0:1, 1:2])
                    vector.tensor_copy(ACC[0:1, 0:1], ACC[0:1, 1:2])
                    vector.tensor_scalar(ACC[0:1, 2:3], ACC[0:1, 1:2], CST[0:1, 3:4],
                                         CST[0:1, 0:1], ALU.mult, ALU.max)
                    vector.tensor_copy(ACC[0:1, 0:1], ACC[0:1, 2:3])
                    vector.tensor_copy(ACC[0:1, 0:1], ACC[0:1, 2:3])
                    vector.reciprocal(ACC[0:1, 3:4], ACC[0:1, 2:3])
                    vector.tensor_copy(ACC[0:1, 0:1], ACC[0:1, 3:4])
                    inc("dv", vector.tensor_copy(ACC[0:1, 0:1], ACC[0:1, 3:4]))
                ckm2 = CNT["dv"]
                @block.tensor
                def _(tensor):
                    wait(tensor, "dv", ckm2)
                    inc("pe", tensor.matmul(PSR[:, 0:1], ONER[:], ACC[0:1, 3:4],
                                            start=True, stop=True))
                    inc("pe", tensor.matmul(PSR[:, 1:2], ONER[:], ACC[0:1, 2:3],
                                            start=True, stop=True))
                ckm3 = CNT["pe"]
                @block.scalar
                def _(scalar):
                    wait(scalar, "pe", ckm3)
                    scalar.copy(SCL[:, scol:scol + 1], PSR[:, 0:1])
                    scalar.copy(SCL[:, mcol:mcol + 1], PSR[:, 1:2])
                    scalar.copy(SCL[:, 6:7], SCL[:, scol:scol + 1])
                    inc("ac", scalar.copy(SCL[:, 6:7], SCL[:, scol:scol + 1]))
                cksc = CNT["ac"]
                for fi in range(n_fch):
                    g0 = CNT["sy"]
                    wfree = wf32_free[0]
                    @block.sync
                    def _(sync):
                        wait(sync, "dv", wfree)
                        sync.dma_start(WF32[:, 0:csz],
                                       w_ap[128 * fi:128 * (fi + 1), :]).then_inc(SEM["sy"], 16)
                        CNT["sy"] += 16
                    @block.vector
                    def _(vector):
                        wait(vector, "sy", g0 + 16)
                        wait(vector, "ac", cksc)
                        wait(vector, "ac", wq32_free[0])
                        inc("dv", vector.tensor_scalar(WQ32[:, 0:csz], WF32[:, 0:csz],
                                                       SCL[:, scol:scol + 1], None, ALU.mult))
                    wf32_free[0] = CNT["dv"]
                    ckq1 = CNT["dv"]
                    @block.scalar
                    def _(scalar):
                        wait(scalar, "dv", ckq1)
                        scalar.copy(WI32[:, 0:csz], WQ32[:, 0:csz])
                        inc("ac", scalar.copy(WQ32[:, 0:csz], WI32[:, 0:csz]))
                    ckq2 = CNT["ac"]
                    wq32_free[0] = ckq2
                    wfree2 = wld_free[0]
                    @block.vector
                    def _(vector):
                        wait(vector, "ac", ckq2)
                        wait(vector, "pe", wfree2)
                        inc("dv", vector.tensor_scalar(WLD[:, 0:csz], WQ32[:, 0:csz],
                                                       1.0, -1.0, ALU.min, ALU.max))
                    ckq3 = CNT["dv"]
                    @block.tensor
                    def _(tensor):
                        wait(tensor, "dv", ckq3)
                    _trans_tile(fi, n_cch, dst_fn)

            ffn_quant(wg_d, 0, 3, 32, C,
                      lambda fi, ci: wgT_s[ci, fi // 4, :, 128 * (fi % 4):128 * (fi % 4) + 128])
            ffn_quant(wu_d, 1, 4, 32, C,
                      lambda fi, ci: wuT_s[ci, fi // 4, :, 128 * (fi % 4):128 * (fi % 4) + 128])
            ffn_quant(wd_d, 2, 5, 8, I_FFN,
                      lambda fi, ci: wdT_s[ci, fi // 4, :, 128 * (fi % 4):128 * (fi % 4) + 128])

            @block.vector
            def _(vector):
                vector.tensor_tensor(WSC[:, 2:3], SCL[:, 3:4], SCL[:, 3:4], ALU.mult)
                vector.tensor_tensor(WSC[:, 1:2], SCL[:, 5:6], CST[:, 1:2], ALU.mult)
                vector.tensor_copy(WSC[:, 3:4], WSC[:, 2:3])
                vector.tensor_copy(WSC[:, 3:4], WSC[:, 2:3])
                vector.tensor_tensor(WSC[:, 0:1], WSC[:, 2:3], SCL[:, 4:5], ALU.mult)
                vector.tensor_copy(WSC[:, 3:4], WSC[:, 0:1])
                inc("dv", vector.tensor_copy(WSC[:, 3:4], WSC[:, 0:1]))
            ck_wst = last_store[0]

            @block.sync
            def _(sync):
                wait(sync, "vq", ck_wst)
                for ci in range(8):
                    for fo in range(2):
                        sync.dma_start(PWT[:, 1024 * ci + 512 * fo:1024 * ci + 512 * fo + 512],
                                       projwT_s[ci, fo, :, :]).then_inc(SEM["sy"], 16)
                        CNT["sy"] += 16
            ck_pwt = CNT["sy"]

        BATCH_GATE = [0]
        OUT_ST_B = [0]
        H_RD_LAST = [0]
        for b in range(1 if STOP else BPC):
            sfx = f"_b{b}"
            stp = ExitStack()
            PSA = stp.enter_context(nc.psum_tensor("PSA" + sfx, [128, 2048], F32))
            PSB = stp.enter_context(nc.psum_tensor("PSB" + sfx, [128, 2048], F32))
            PSAH = PSA.ap().bitcast(F16)
            PSBH = PSB.ap().bitcast(F16)
            with stp:
                YP = stp.enter_context(nc.sbuf_tensor("YP" + sfx, [128, 8 * 2048], F16))

                sta = ExitStack()
                Ea = sta.enter_context
                with sta:
                    XFM = Ea(nc.sbuf_tensor("XFM" + sfx, [128, 8 * 2048], F16))
                    XLD = Ea(nc.sbuf_tensor("XLD" + sfx, [128, 2 * 1024], F16))
                    QWTB = Ea(nc.sbuf_tensor("QWTB" + sfx, [128, 3072], F16))
                    QREP = Ea(nc.sbuf_tensor("QREP" + sfx, [128, 2 * 2048], F32))
                    KP = Ea(nc.sbuf_tensor("KP" + sfx, [128, 2 * 1024], F32))
                    KPN = Ea(nc.sbuf_tensor("KPN" + sfx, [128, 2 * 1024], F32))
                    VFM = Ea(nc.sbuf_tensor("VFM" + sfx, [128, 2048], F32))
                    VFM6 = Ea(nc.sbuf_tensor("VFM6" + sfx, [128, 2048], F16))
                    VT1 = Ea(nc.sbuf_tensor("VT1" + sfx, [128, 2 * 1056], F16))
                    LOC = Ea(nc.sbuf_tensor("LOC" + sfx, [128, 2048], F32))
                    LOC2 = Ea(nc.sbuf_tensor("LOC2" + sfx, [128, 2048], F32))
                    XLD32 = LOC2.ap()[:, 0:1024]
                    DS = Ea(nc.sbuf_tensor("DS" + sfx, [128, 2048], F16))
                    DD = Ea(nc.sbuf_tensor("DD" + sfx, [128, 2 * 2048], F16))
                    DA = Ea(nc.sbuf_tensor("DA" + sfx, [128, 2 * 2048], F16))
                    KERN = Ea(nc.sbuf_tensor("KERN" + sfx, [128, 2 * 2048], F16))
                    GT = Ea(nc.sbuf_tensor("GT" + sfx, [128, 1024], F16))

                    RCP = Ea(nc.sbuf_tensor("RCP" + sfx, [128, 96], F32))


                    a1ev = {}
                    a1t = {}
                    a1cv = {}
                    for tc in range(16):
                        g0 = CNT["sy"]
                        @block.sync
                        def _(sync, tc=tc):
                            if tc == 0:
                                if b == 0:
                                    wait(sync, "vq", ck_wst)
                                else:
                                    wait(sync, "dv", BATCH_GATE[0])
                                    wait(sync, "vq", OUT_ST_B[0])
                            if tc >= 1:
                                wait(sync, "dv", a1cv[tc - 1])
                            sync.dma_start(XLD32,
                                           x_in[b, 128 * tc:128 * (tc + 1), :]).then_inc(SEM["sy"], 16)
                            CNT["sy"] += 16
                        @block.vector
                        def _(vector, tc=tc):
                            wait(vector, "sy", g0 + 16)
                            if tc >= 2:
                                wait(vector, "pe", a1t[tc - 2])
                            inc("dv", vector.tensor_copy(
                                XLD[:, 1024 * (tc % 2):1024 * (tc % 2) + 1024],
                                XLD32))
                        a1cv[tc] = CNT["dv"]
                        @block.tensor
                        def _(tensor, tc=tc):
                            wait(tensor, "dv", a1cv[tc])
                            wait(tensor, "sy", ck_pwt)
                            if b > 0:
                                wait(tensor, "dv", BATCH_GATE[0])
                            if tc >= 1:
                                wait(tensor, "ac", a1ev[tc - 1])
                            for ci in range(8):
                                inc("pe", tensor.transpose(
                                    PSBH[:, 128 * ci:128 * (ci + 1)],
                                    XLD[:, 1024 * (tc % 2) + 128 * ci:1024 * (tc % 2) + 128 * (ci + 1)],
                                    ID16[:]))
                        a1t[tc] = CNT["pe"]
                        ckt = CNT["pe"]
                        @block.scalar
                        def _(scalar, tc=tc):
                            wait(scalar, "pe", ckt)
                            for ci in range(8):
                                ins = scalar.copy(
                                    XFM[:, 2048 * ci + 128 * tc:2048 * ci + 128 * (tc + 1)],
                                    PSBH[:, 128 * ci:128 * (ci + 1)])
                            inc("ac", ins)
                        a1ev[tc] = CNT["ac"]
                    ck_a1 = CNT["ac"]
                    ATT_DONE = [0]
                    QKV_MM_LAST = [0]
                    QKV_EV = {}
                    for hp in range(8):
                        if STOP and hp > 0:
                            continue
                        @block.sync
                        def _(sync, hp=hp):
                            if hp > 0:
                                wait(sync, "pe", QKV_MM_LAST[0])
                            elif b > 0:
                                wait(sync, "dv", BATCH_GATE[0])
                            for p3 in range(3):
                                for ci in range(8):
                                    sync.dma_start(
                                        QWTB[:, 1024 * p3 + 128 * ci:1024 * p3 + 128 * (ci + 1)],
                                        qkvwT_s[ci, p3 * 8 + hp, :, :]).then_inc(SEM["sy"], 16)
                                    CNT["sy"] += 16
                        s1 = CNT["sy"]

                        for p3 in range(3):
                            @block.tensor
                            def _(tensor, hp=hp, p3=p3):
                                wait(tensor, "sy", s1)
                                wait(tensor, "ac", ck_a1)
                                if p3 > 0:
                                    wait(tensor, "ac", QKV_EV[(b, hp, p3 - 1)])
                                elif hp > 0:
                                    wait(tensor, "ac", QKV_EV[(b, hp - 1, 2)])
                                    wait(tensor, "dv", ATT_DONE[0])
                                elif b > 0:
                                    wait(tensor, "dv", BATCH_GATE[0])
                                for t4 in range(4):
                                    for ci in range(8):
                                        inc("pe", tensor.matmul(
                                            PSA[:, 512 * t4:512 * (t4 + 1)],
                                            QWTB[:, 1024 * p3 + 128 * ci:1024 * p3 + 128 * (ci + 1)],
                                            XFM[:, 2048 * ci + 512 * t4:2048 * ci + 512 * (t4 + 1)],
                                            start=(ci == 0), stop=(ci == 7)))
                            QKV_MM_LAST[0] = CNT["pe"]
                            ckmm = CNT["pe"]
                            fch = p3 * 8 + hp
                            @block.scalar
                            def _(scalar, hp=hp, p3=p3, fch=fch):
                                wait(scalar, "pe", ckmm)
                                if p3 == 0:
                                    for hh2 in range(2):
                                        bia = QKB[64 * hh2:64 * hh2 + 64, fch:fch + 1]
                                        scalar.activation(
                                            QREP[0:64, 2048 * hh2:2048 * (hh2 + 1)],
                                            PSA[64 * hh2:64 * hh2 + 64, :],
                                            AF.Identity, bias=bia, scale=1.0)
                                        ins = scalar.activation(
                                            QREP[64:128, 2048 * hh2:2048 * (hh2 + 1)],
                                            PSA[64 * hh2:64 * hh2 + 64, :],
                                            AF.Identity, bias=bia, scale=1.0)
                                    inc("ac", ins)
                                elif p3 == 1:
                                    for hh2 in range(2):
                                        bia = QKB[64 * hh2:64 * hh2 + 64, fch:fch + 1]
                                        bian = QKBN[64 * hh2:64 * hh2 + 64, fch:fch + 1]
                                        for par in range(2):
                                            src = PSA.ap().rearrange(
                                                "p (t two) -> p two t", two=2)[
                                                64 * hh2:64 * hh2 + 64, par, :]
                                            scalar.activation(
                                                KP[64 * par:64 * par + 64, 1024 * hh2:1024 * (hh2 + 1)],
                                                src, AF.Identity, bias=bia, scale=1.0)
                                            ins = scalar.activation(
                                                KPN[64 * par:64 * par + 64, 1024 * hh2:1024 * (hh2 + 1)],
                                                src, AF.Identity, bias=bian, scale=-1.0)
                                    inc("ac", ins)
                                else:
                                    scalar.activation(VFM[:], PSA[:], AF.Identity,
                                                      bias=QKB[:, fch:fch + 1], scale=1.0)
                                    inc("ac", scalar.copy(VFM6[:], VFM[:]))
                            QKV_EV[(b, hp, p3)] = CNT["ac"]
                        ck_qkvev = CNT["ac"]
                        @block.tensor
                        def _(tensor, hp=hp):
                            wait(tensor, "ac", ck_qkvev)
                            for jc in range(16):
                                dst = PSBH[:, 128 * jc:128 * (jc + 1)] if jc < 8 else \
                                    PSAH[:, 128 * (jc - 8):128 * (jc - 7)]
                                inc("pe", tensor.transpose(dst, VFM6[:, 128 * jc:128 * (jc + 1)],
                                                           ID16[:]))
                        ck_vt = CNT["pe"]
                        @block.scalar
                        def _(scalar, hp=hp):
                            wait(scalar, "pe", ck_vt)
                            for jc in range(16):
                                psrc = PSBH[:, 128 * jc:128 * (jc + 1)] if jc < 8 else \
                                    PSAH[:, 128 * (jc - 8):128 * (jc - 7)]
                                for hh2 in range(2):
                                    ins = scalar.copy(
                                        VT1[:, 1056 * hh2 + 66 * jc:1056 * hh2 + 66 * jc + 64],
                                        psrc[:, 64 * hh2:64 * hh2 + 64])
                            inc("ac", ins)
                        ck_vtev = CNT["ac"]
                        @block.vector
                        def _(vector, hp=hp):
                            wait(vector, "ac", ck_vtev)
                            for hh2 in range(2):
                                for jc in range(16):
                                    vector.memset(
                                        VT1[:, 1056 * hh2 + 66 * jc + 64:1056 * hh2 + 66 * jc + 65],
                                        1.0)
                                    vector.memset(
                                        VT1[:, 1056 * hh2 + 66 * jc + 65:1056 * hh2 + 66 * jc + 66],
                                        0.0)
                            vector.tensor_scalar(LOC[:], VFM[:], DWW[:, 3 * hp + 1:3 * hp + 2],
                                                 None, ALU.mult)
                            vector.scalar_tensor_tensor(LOC2[:, 1:2048], VFM[:, 0:2047],
                                                        DWW[:, 3 * hp:3 * hp + 1], LOC[:, 1:2048],
                                                        ALU.mult, ALU.add)
                            vector.tensor_copy(LOC2[:, 0:1], LOC[:, 0:1])
                            vector.scalar_tensor_tensor(LOC[:, 0:2047], VFM[:, 1:2048],
                                                        DWW[:, 3 * hp + 2:3 * hp + 3], LOC2[:, 0:2047],
                                                        ALU.mult, ALU.add)
                            inc("dv", vector.tensor_copy(LOC[:, 2047:2048], LOC2[:, 2047:2048]))
                        ck_dwc = CNT["dv"]
                        for hh in range(2):
                            if STOP and (b > 0 or hp > 0 or hh > 0):
                                continue
                            qr = QREP.ap()[:, 2048 * hh:2048 * (hh + 1)]
                            kp_ = KP.ap()[:, 1024 * hh:1024 * (hh + 1)]
                            kpn_ = KPN.ap()[:, 1024 * hh:1024 * (hh + 1)]

                            bd, ba, bp = CNT["dv"], CNT["ac"], CNT["pe"]
                            pe_g = lambda jt, g, bp=bp: bp + jt * 65 + g + 1
                            pe_av = lambda jt, bp=bp: bp + (jt + 1) * 65
                            dv_t = lambda n, bd=bd: bd + n + 1
                            ac_ev = lambda jt, i, ba=ba: ba + jt * (ACT_T + 4) + i + 1
                            ac_exp = lambda jt, c2, ba=ba: ba + jt * (ACT_T + 4) + ACT_T + c2 + 1

                            @block.vector
                            def _(vector):
                                wait(vector, "ac", ck_qkvev)
                                n = 0
                                for jt in range(NJT):
                                    for idx in range(DVT):
                                        buf = n % 2
                                        col = jt * 64 + idx
                                        if n >= 2:
                                            jt2, i2 = divmod(n - 2, DVT)
                                            wait(vector, "pe", pe_g(jt2, POS_D[i2]))
                                        else:
                                            wait(vector, "pe", bp)
                                        vector.tensor_scalar(DS[:], qr, kp_[:, col:col + 1],
                                                             None, ALU.subtract)
                                        inc("dv", vector.tensor_scalar(
                                            DD.ap().bitcast(I32)[:, 1024 * buf:1024 * (buf + 1)],
                                            DS.ap().bitcast(I32)[:, 0:1024],
                                            M7[:], None, ALU.bitwise_and))
                                        n += 1

                            @block.scalar
                            def _(scalar):
                                m = 0
                                for jt in range(NJT):
                                    for idx in range(ACT_T):
                                        buf = m % 2
                                        col = jt * 64 + DVT + idx
                                        if m >= 2:
                                            jt2, i2 = divmod(m - 2, ACT_T)
                                            wait(scalar, "pe", pe_g(jt2, POS_A[i2]))
                                        else:
                                            wait(scalar, "pe", bp)
                                        inc("ac", scalar.activation(
                                            DA[:, 2048 * buf:2048 * (buf + 1)], qr,
                                            AF.Abs, bias=kpn_[:, col:col + 1], scale=1.0))
                                        m += 1
                                    wait(scalar, "pe", pe_g(jt, 63))
                                    if jt >= 2:
                                        wait(scalar, "pe", pe_av(jt - 2))
                                    kb = jt % 2
                                    for c2 in range(4):
                                        inc("ac", scalar.activation(
                                            KERN[:, 2048 * kb + 512 * c2:2048 * kb + 512 * (c2 + 1)],
                                            PSA[:, 512 * c2:512 * (c2 + 1)],
                                            AF.Exp, bias=CST[:, 4:5], scale=-1.0 / 16.0))

                            @block.tensor
                            def _(tensor, hh=hh):
                                wait(tensor, "dv", ck_dwc)
                                ndc = nac = 0
                                for jt in range(NJT):
                                    for pos, (stype, sidx) in enumerate(MIX):
                                        if stype == "d":
                                            jp = sidx
                                            wait(tensor, "dv", dv_t(ndc))
                                            src = DD.ap()[:, 2048 * (ndc % 2):2048 * (ndc % 2) + 2048]
                                            ndc += 1
                                        else:
                                            jp = DVT + sidx
                                            wait(tensor, "ac", ac_ev(jt, sidx))
                                            src = DA.ap()[:, 2048 * (nac % 2):2048 * (nac % 2) + 2048]
                                            nac += 1
                                        if pos == 0 and jt >= 1:
                                            wait(tensor, "ac", ac_exp(jt - 1, 3))
                                        for c2 in range(4):
                                            mm = tensor.matmul(
                                                PSA[:, 512 * c2:512 * (c2 + 1)],
                                                MASKS[:, 128 * jp:128 * (jp + 1)],
                                                src[:, 512 * c2:512 * (c2 + 1)],
                                                start=(pos == 0), stop=(pos == 63))
                                        inc("pe", mm)
                                    wait(tensor, "ac", ac_exp(jt, 3))
                                    kb = jt % 2
                                    if jt == 0:
                                        for bank in range(4):
                                            tensor.matmul(
                                                PSB[:, 512 * bank:512 * bank + 512],
                                                ZRO16[:], MASKS[:, 0:512],
                                                start=True, stop=False,
                                                skip_group_check=True)
                                    for g4 in range(16):
                                        bank, off = g4 // 4, 66 * (g4 % 4)
                                        mm = tensor.matmul(
                                            PSB[:, 512 * bank + off:512 * bank + off + 66],
                                            KERN[:, 2048 * kb + 128 * g4:2048 * kb + 128 * (g4 + 1)],
                                            VT1[:, 1056 * hh + 66 * jt:1056 * hh + 66 * jt + 66],
                                            start=False,
                                            stop=(jt == NJT - 1),
                                            skip_group_check=True)
                                    inc("pe", mm)

                            @block.scalar
                            def _(scalar, hh=hh):
                                wait(scalar, "pe", pe_av(NJT - 1))
                                for g4 in range(16):
                                    bank, off = g4 // 4, 66 * (g4 % 4)
                                    ins = scalar.activation(
                                        RCP[:, g4:g4 + 1],
                                        PSB[:, 512 * bank + off + 64:512 * bank + off + 65],
                                        AF.Identity, bias=CST[:, 6:7], scale=1.0)
                                inc("ac", ins)
                            ck_sums = CNT["ac"]
                            @block.vector
                            def _(vector):
                                wait(vector, "ac", ck_sums)
                                vector.reciprocal(RCP[:, 32:48], RCP[:, 0:16])
                                vector.tensor_copy(DS[:, 0:16], RCP[:, 32:48])
                                inc("dv", vector.tensor_copy(DS[:, 32:96], MASKS[:, 0:64]))
                            ck_rcp = CNT["dv"]
                            @block.scalar
                            def _(scalar, hh=hh):
                                wait(scalar, "dv", ck_rcp)
                                scalar.copy(
                                    RCP.ap()[:, 64:96].rearrange(
                                        "p (c two) -> p two c", two=2)[:, 0, :],
                                    RCP[:, 32:48])
                                for g4 in range(15):
                                    bank, off = g4 // 4, 66 * (g4 % 4)
                                    scalar.activation(
                                        GT[:, 64 * g4:64 * (g4 + 1)],
                                        PSB[:, 512 * bank + off:512 * bank + off + 64],
                                        AF.Identity, bias=CST[:, 5:6],
                                        scale=RCP[:, 64 + 2 * g4:65 + 2 * g4])
                                scalar.copy(LOC2[:, 0:64], PSB[:, 1734:1798])
                                scalar.copy(DS[:, 0:16], LOC2[:, 48:64])
                                inc("ac", scalar.copy(DS[:, 0:16], LOC2[:, 48:64]))
                            ck_gt = CNT["ac"]
                            @block.vector
                            def _(vector):
                                wait(vector, "ac", ck_gt)
                                vector.tensor_scalar(GT[:, 960:1024], LOC2[:, 0:64],
                                                     RCP[:, 47:48], None, ALU.mult)
                                vector.tensor_copy(DS[:, 16:80], GT[:, 960:1024])
                                inc("dv", vector.tensor_copy(DS[:, 16:80],
                                                             GT[:, 960:1024]))
                            ck_gtd = CNT["dv"]
                            if STOP:
                                @block.sync
                                def _(sync):
                                    wait(sync, "ac", ck_gt)
                                    wait(sync, "dv", ck_gtd)
                                    sync.dma_start(out_d[0, 0:128, 0:512],
                                                   GT.ap().bitcast(F32)[:, 0:512]).then_inc(SEM["vq"], 16)
                                    sync.dma_start(out_d[0, 128:256, 0:64],
                                                   RCP[:, 0:64]).then_inc(SEM["vq"], 16)
                                    sync.dma_start(out_d[0, 256:384, 0:512],
                                                   KERN.ap()[:, 2048:4096].bitcast(F32)[:, 0:512]).then_inc(SEM["vq"], 16)
                                    CNT["vq"] += 48
                                @block.scalar
                                def _(scalar):
                                    wait(scalar, "pe", pe_av(NJT - 1))
                                    inc("ac", scalar.copy(LOC2[:], PSB[:]))
                                _ckb3 = CNT["ac"]
                                @block.sync
                                def _(sync):
                                    wait(sync, "ac", _ckb3)
                                    for qq in range(4):
                                        sync.dma_start(out_d[0, 384 + 128 * qq:512 + 128 * qq, 0:512],
                                                       LOC2[:, 512 * qq:512 * (qq + 1)]).then_inc(SEM["vq"], 16)
                                        CNT["vq"] += 16
                            if DEBUG and b == 0 and hp == 7 and hh == 1:
                                @block.sync
                                def _(sync):
                                    wait(sync, "ac", ck_gt)
                                    wait(sync, "dv", ck_gtd)
                                    sync.dma_start(dbg["gtb"], GT[:]).then_inc(SEM["sy"], 16)
                                    CNT["sy"] += 16
                            if DEBUG and b == 0 and hp == 0 and hh == 0:
                                @block.scalar
                                def _(scalar):
                                    wait(scalar, "pe", pe_av(NJT - 1))
                                    wait(scalar, "dv", ck_gtd)
                                    inc("ac", scalar.copy(LOC2[:, 0:1024], PSB[:, 0:1024]))
                                _ckavd = CNT["ac"]
                                @block.sync
                                def _(sync):
                                    wait(sync, "ac", _ckavd)
                                    wait(sync, "dv", ck_gtd)
                                    sync.dma_start(dbg["avps"], LOC2[:, 0:1024]).then_inc(SEM["sy"], 16)
                                    sync.dma_start(dbg["kern"], KERN[:, 2048:4096]).then_inc(SEM["sy"], 16)
                                    sync.dma_start(dbg["vt1"], VT1[:]).then_inc(SEM["sy"], 16)
                                    sync.dma_start(dbg["gt"], GT[:]).then_inc(SEM["sy"], 16)
                                    CNT["sy"] += 64
                            @block.tensor
                            def _(tensor):
                                wait(tensor, "ac", ck_gt)
                                wait(tensor, "dv", ck_gtd)
                                for tb in range(8):
                                    inc("pe", tensor.transpose(PSAH[:, 128 * tb:128 * (tb + 1)],
                                                               GT[:, 128 * tb:128 * (tb + 1)],
                                                               ID16[:]))
                            ck_gtt = CNT["pe"]

                            @block.vector
                            def _(vector, hh=hh, hp=hp):
                                wait(vector, "pe", ck_gtt)
                                row0 = 64 * hh
                                for tb in range(8):
                                    for half in range(2):
                                        ic = 2 * tb + half
                                        ins = vector.scalar_tensor_tensor(
                                            YP[row0:row0 + 64,
                                               2048 * hp + 128 * ic:2048 * hp + 128 * (ic + 1)],
                                            PSAH[64 * half:64 * half + 64, 128 * tb:128 * (tb + 1)],
                                            DWB[row0:row0 + 64, hp:hp + 1],
                                            LOC[row0:row0 + 64, 128 * ic:128 * (ic + 1)],
                                            ALU.add, ALU.add)
                                inc("dv", ins)
                            ATT_DONE[0] = CNT["dv"]
                    ck_attn_all = CNT["dv"]
                    if DEBUG and b == 0:
                        @block.sync
                        def _(sync):
                            wait(sync, "dv", ck_attn_all)
                            sync.dma_start(dbg["yp"], YP[:]).then_inc(SEM["sy"], 16)
                            CNT["sy"] += 16

                if STOP:
                    continue
                if ATTONLY:
                    BATCH_GATE[0] = ck_attn_all
                    continue
                stf = ExitStack()
                Ef = stf.enter_context
                with stf:
                    Y1B = Ef(nc.sbuf_tensor("Y1B" + sfx, [128, 4096], F32))
                    X1Q = Ef(nc.sbuf_tensor("X1Q" + sfx, [128, 4096], F16))
                    XLD2 = Ef(nc.sbuf_tensor("XLD2" + sfx, [128, 1024], F32))
                    XPB = Ef(nc.sbuf_tensor("XPB" + sfx, [128, 1024], F32))
                    RES = Ef(nc.sbuf_tensor("RES" + sfx, [128, 1024], F32))
                    CEN = Ef(nc.sbuf_tensor("CEN" + sfx, [128, 1024], F32))
                    QI = Ef(nc.sbuf_tensor("QI" + sfx, [128, 2048], I32))
                    QF = Ef(nc.sbuf_tensor("QF" + sfx, [128, 2048], F32))
                    EG = Ef(nc.sbuf_tensor("EG" + sfx, [128, 512], F32))
                    SQ2 = Ef(nc.sbuf_tensor("SQ2" + sfx, [128, 512], F32))
                    UE = Ef(nc.sbuf_tensor("UE" + sfx, [128, 512], F32))
                    HHB = Ef(nc.sbuf_tensor("HHB" + sfx, [128, 512], F32))
                    HRD = Ef(nc.sbuf_tensor("HRD" + sfx, [128, 4096], F32))
                    DNE = Ef(nc.sbuf_tensor("DNE" + sfx, [128, 1024], F32))
                    X2Q = Ef(nc.sbuf_tensor("X2Q" + sfx, [128, 8192], F16))
                    WGB = Ef(nc.sbuf_tensor("WGB" + sfx, [128, 4096], F16))
                    WUB = Ef(nc.sbuf_tensor("WUB" + sfx, [128, 4096], F16))
                    WDB = Ef(nc.sbuf_tensor("WDB" + sfx, [128, 8192], F16))
                    COLS = Ef(nc.sbuf_tensor("COLS" + sfx, [128, 16], F32))
                    COLS2 = Ef(nc.sbuf_tensor("COLS2" + sfx, [128, 16], F32))
                    COLS3 = Ef(nc.sbuf_tensor("COLS3" + sfx, [128, 16], F32))
                    OUTB = Ef(nc.sbuf_tensor("OUTB" + sfx, [128, 1024], F32))

                    if b == 0:
                        OUT_ST = [0]
                        PROJ_EV = {}
                        PRJDBG = [0]
                        DNEDBG = [0]
                        DN_EV = [0]
                        X2_EV = [0]
                        WDB_MM = [0]
                        QF_FREE = [0]
                        H_EV = [0]
                        H_WR = [0]
                    for tc4 in range(4):
                        x1tr = {}
                        for tsl in range(4):
                            ts_g = 4 * tc4 + tsl
                            g0 = CNT["sy"]
                            @block.sync
                            def _(sync, ts_g=ts_g, tc4=tc4, tsl=tsl):
                                wait(sync, "dv", ck_attn_all)
                                if tsl > 0 or tc4 > 0:
                                    _pd, _pa = PROJ_EV[(tc4, tsl - 1) if tsl > 0
                                                       else (tc4 - 1, 3)]
                                    wait(sync, "dv", _pd)
                                    wait(sync, "ac", _pa)
                                sync.dma_start(XLD2[:],
                                               x_in[b, 128 * ts_g:128 * (ts_g + 1), :]
                                               ).then_inc(SEM["sy"], 16)
                                CNT["sy"] += 16
                            @block.tensor
                            def _(tensor, ts_g=ts_g, tsl=tsl, tc4=tc4):
                                wait(tensor, "dv", ck_attn_all)
                                if tsl > 0 or tc4 > 0:
                                    _pd, _pa = PROJ_EV[(tc4, tsl - 1) if tsl > 0
                                                       else (tc4 - 1, 3)]
                                    wait(tensor, "dv", _pd)
                                    wait(tensor, "ac", _pa)
                                    if tsl == 0:
                                        wait(tensor, "dv", DN_EV[0])
                                    if DEBUG and PRJDBG[0]:
                                        wait(tensor, "vq", PRJDBG[0])
                                elif b > 0:
                                    wait(tensor, "dv", BATCH_GATE[0])
                                for fo in range(2):
                                    for ci in range(8):
                                        inc("pe", tensor.matmul(
                                            PSA[:, 512 * fo:512 * (fo + 1)],
                                            YP[:, 2048 * ci + 128 * ts_g:2048 * ci + 128 * (ts_g + 1)],
                                            PWT[:, 1024 * ci + 512 * fo:1024 * ci + 512 * fo + 512],
                                            start=(ci == 0), stop=(ci == 7)))
                            ckpm = CNT["pe"]
                            @block.vector
                            def _(vector):
                                wait(vector, "sy", g0 + 16)
                                wait(vector, "pe", ckpm)
                                vector.tensor_tensor(XPB[:], XLD2[:], PRJB[:], ALU.add)
                                vector.scalar_tensor_tensor(RES[:], PSA[:, 0:1024], 0.0,
                                                            XPB[:], ALU.add, ALU.add)
                                vector.tensor_reduce(COLS[:, 0:1], RES[:], AX.X, ALU.add)
                                vector.tensor_copy(COLS3[:], COLS[:])
                                inc("dv", vector.tensor_copy(COLS3[:], COLS[:]))
                            ck_rsum = CNT["dv"]
                            @block.scalar
                            def _(scalar):
                                wait(scalar, "dv", ck_rsum)
                                scalar.activation(COLS[:, 6:7], COLS[:, 0:1],
                                                  AF.Identity, bias=CST[:, 5:6],
                                                  scale=-1.0 / 1024.0)
                                scalar.copy(COLS3[:], COLS[:])
                                scalar.copy(COLS3[:], COLS[:])
                                inc("ac", scalar.activation(CEN[:], RES[:], AF.Identity,
                                                            bias=COLS[:, 6:7], scale=1.0))
                            ck_cen = CNT["ac"]
                            if DEBUG and b == 0 and ts_g == 0:
                                @block.scalar
                                def _(scalar):
                                    wait(scalar, "ac", ck_cen)
                                    inc("ac", scalar.copy(OUTB[:], PSA[:, 0:1024]))
                                _ckprj = CNT["ac"]
                                @block.sync
                                def _(sync):
                                    wait(sync, "ac", _ckprj)
                                    sync.dma_start(dbg["prj0"], OUTB[:]).then_inc(SEM["vq"], 16)
                                    sync.dma_start(dbg["xpb0"], XPB[:]).then_inc(SEM["vq"], 16)
                                    sync.dma_start(dbg["cen0"], CEN[:]).then_inc(SEM["vq"], 16)
                                    CNT["vq"] += 48
                                PRJDBG[0] = CNT["vq"]
                            @block.scalar
                            def _(scalar):
                                scalar.activation(RES[:], CEN[:], AF.Square,
                                                  bias=CST[:, 5:6], scale=1.0,
                                                  accum_out=COLS[:, 1:2])
                                scalar.copy(COLS3[:], COLS[:])
                                inc("ac", scalar.copy(COLS3[:], COLS[:]))
                            ck_var = CNT["ac"]
                            PROJ_EV[(tc4, tsl)] = (ck_rsum, ck_var)
                            @block.vector
                            def _(vector):
                                wait(vector, "ac", ck_var)
                                vector.tensor_scalar(COLS2[:, 6:7], COLS[:, 1:2], 1.0 / 1024.0,
                                                     1e-5, ALU.mult, ALU.add)
                                vector.tensor_copy(COLS3[:], COLS[:])
                                vector.tensor_copy(COLS3[:], COLS[:])
                                vector.reciprocal(COLS[:, 2:3], COLS2[:, 6:7])
                                vector.tensor_copy(COLS3[:], COLS[:])
                                inc("dv", vector.tensor_copy(COLS3[:], COLS[:]))
                            ck_rv = CNT["dv"]
                            @block.scalar
                            def _(scalar, tsl=tsl):
                                wait(scalar, "dv", ck_rv)
                                scalar.sqrt(COLS[:, 3:4], COLS[:, 2:3])
                                scalar.copy(COLS3[:], COLS[:])
                                scalar.copy(COLS3[:], COLS[:])
                                scalar.activation(
                                    Y1B[:, 1024 * tsl:1024 * (tsl + 1)], CEN[:],
                                    AF.Identity, bias=CST[:, 5:6], scale=COLS[:, 3:4])
                                inc("ac", scalar.copy(COLS3[:],
                                                      Y1B[:, 1024 * tsl + 1008:1024 * (tsl + 1)]))
                            ck_y1 = CNT["ac"]
                            @block.vector
                            def _(vector, tsl=tsl):
                                wait(vector, "ac", ck_y1)
                                vector.tensor_reduce(COLS[:, 4:5],
                                                     Y1B[:, 1024 * tsl:1024 * (tsl + 1)],
                                                     AX.X, ALU.max, apply_absolute_value=True)
                                vector.tensor_copy(COLS3[:], COLS[:])
                                vector.tensor_copy(COLS3[:], COLS[:])
                                vector.tensor_scalar(COLS2[:, 7:8], COLS[:, 4:5], 1e-5,
                                                     None, ALU.max)
                                vector.tensor_copy(COLS3[:], COLS[:])
                                vector.tensor_copy(COLS3[:], COLS[:])
                                vector.reciprocal(COLS[:, 5:6], COLS2[:, 7:8])
                                vector.tensor_copy(COLS[:, 8 + tsl:9 + tsl], COLS2[:, 7:8])
                                inc("dv", vector.tensor_copy(COLS3[:], COLS[:]))
                            ck_qs = CNT["dv"]
                            @block.scalar
                            def _(scalar, tsl=tsl):
                                wait(scalar, "dv", ck_qs)
                                scalar.activation(COLS[:, 7:8], COLS[:, 5:6],
                                                  AF.Identity, bias=CST[:, 5:6],
                                                  scale=127.0)
                                scalar.copy(COLS3[:], COLS[:])
                                scalar.copy(COLS3[:], COLS[:])
                                scalar.activation(QF[:, 0:1024],
                                                  Y1B[:, 1024 * tsl:1024 * (tsl + 1)],
                                                  AF.Identity, bias=CST[:, 5:6],
                                                  scale=COLS[:, 7:8])
                                scalar.copy(QI[:, 0:1024], QF[:, 0:1024])
                                inc("ac", scalar.copy(X1Q.ap()[:, 1024 * tsl:1024 * (tsl + 1)],
                                                      QI[:, 0:1024]))
                            QF_FREE[0] = CNT["ac"]
                            ck_x1 = CNT["ac"]
                            @block.tensor
                            def _(tensor, tsl=tsl):
                                wait(tensor, "ac", ck_x1)
                                for ci in range(8):
                                    inc("pe", tensor.transpose(
                                        PSBH[:, 128 * ci:128 * (ci + 1)],
                                        X1Q[:, 1024 * tsl + 128 * ci:1024 * tsl + 128 * (ci + 1)],
                                        ID16[:]))
                            ck_x1t = CNT["pe"]
                            @block.scalar
                            def _(scalar, tsl=tsl):
                                wait(scalar, "pe", ck_x1t)
                                inc("ac", scalar.copy(X1Q[:, 1024 * tsl:1024 * (tsl + 1)],
                                                      PSBH[:, 0:1024]))
                            x1tr[tsl] = CNT["ac"]
                        ck_x1all = CNT["ac"]
                        if DEBUG and b == 0 and tc4 == 0:
                            @block.sync
                            def _(sync):
                                wait(sync, "ac", ck_x1all)
                                sync.dma_start(dbg["x1q"], X1Q[:]).then_inc(SEM["sy"], 16)
                                CNT["sy"] += 16

                        h_wr = {}
                        GU_MM = {}
                        for fch in range(8):
                            @block.sync
                            def _(sync, fch=fch):
                                wait(sync, "dv", ck_attn_all)
                                if fch >= 1:
                                    wait(sync, "pe", GU_MM[fch - 1])
                                for ci in range(8):
                                    sync.dma_start(WGB[:, 512 * ci:512 * (ci + 1)],
                                                   wgT_s[ci, fch, :, :]).then_inc(SEM["sy"], 16)
                                    sync.dma_start(WUB[:, 512 * ci:512 * (ci + 1)],
                                                   wuT_s[ci, fch, :, :]).then_inc(SEM["sy"], 16)
                                    CNT["sy"] += 32
                            g1 = CNT["sy"]
                            for tsl in range(4):
                                @block.tensor
                                def _(tensor, fch=fch, tsl=tsl):
                                    wait(tensor, "sy", g1)
                                    wait(tensor, "ac", ck_x1all)
                                    wait(tensor, "dv", H_EV[0])
                                    if fch == 0 and tsl == 0:
                                        wait(tensor, "dv", DN_EV[0])
                                    for ci in range(8):
                                        tensor.matmul(
                                            PSA[:, 1024:1536],
                                            X1Q[:, 1024 * tsl + 128 * ci:1024 * tsl + 128 * (ci + 1)],
                                            WGB[:, 512 * ci:512 * (ci + 1)],
                                            start=(ci == 0), stop=(ci == 7))
                                    for ci in range(8):
                                        mm = tensor.matmul(
                                            PSA[:, 1536:2048],
                                            X1Q[:, 1024 * tsl + 128 * ci:1024 * tsl + 128 * (ci + 1)],
                                            WUB[:, 512 * ci:512 * (ci + 1)],
                                            start=(ci == 0), stop=(ci == 7))
                                    inc("pe", mm)
                                GU_MM[fch] = CNT["pe"]
                                ckgu = CNT["pe"]
                                @block.scalar
                                def _(scalar):
                                    wait(scalar, "pe", ckgu)
                                    scalar.activation(EG[:], PSA[:, 1024:1536], AF.Relu,
                                                      bias=CST[:, 5:6], scale=1.0)
                                    scalar.activation(SQ2[:], EG[:], AF.Square,
                                                      bias=CST[:, 5:6], scale=1.0)
                                    inc("ac", scalar.copy(UE[:], PSA[:, 1536:2048]))
                                ckeg = CNT["ac"]
                                @block.vector
                                def _(vector):
                                    wait(vector, "ac", ckeg)
                                    wait(vector, "vq", H_WR[0])
                                    vector.tensor_tensor(HHB[:], SQ2[:], UE[:], ALU.mult)
                                    vector.tensor_copy(COLS3[:], HHB[:, 496:512])
                                    inc("dv", vector.tensor_copy(COLS3[:],
                                                                 HHB[:, 496:512]))
                                H_EV[0] = CNT["dv"]
                                ckh = CNT["dv"]
                                @block.scalar
                                def _(scalar):
                                    wait(scalar, "dv", ckh)
                                    scalar.copy(UE[:], HHB[:])
                                    scalar.copy(COLS3[:], UE[:, 496:512])
                                    inc("ac", scalar.copy(COLS3[:], UE[:, 496:512]))
                                ckh2 = CNT["ac"]
                                @block.sync
                                def _(sync, fch=fch, tsl=tsl, tc4=tc4, b=b):
                                    wait(sync, "ac", ckh2)
                                    sync.dma_start(h_s[tsl, fch, :, :],
                                                   UE[:]).then_inc(SEM["vq"], 16)
                                    CNT["vq"] += 16
                                    if DEBUG and b == 0 and tc4 == 0 and fch == 0 and tsl == 0:
                                        sync.dma_start(dbg["hhb"],
                                                       UE[:]).then_inc(SEM["vq"], 16)
                                        sync.dma_start(dbg["eg"],
                                                       EG[:]).then_inc(SEM["vq"], 16)
                                        CNT["vq"] += 32
                                H_WR[0] = CNT["vq"]
                                h_wr[(fch, tsl)] = CNT["vq"]

                        for pair in range(2):
                            tsls = (2 * pair, 2 * pair + 1)
                            ckx2e_pair = {}
                            if tc4 == 0 and pair == 0 and b == 0:
                                HRD_FREE = [0]
                                DNE_FREE = [0]
                            for tsl in tsls:
                                tq = tsl % 2
                                g0s = CNT["sy"]
                                @block.sync
                                def _(sync, tsl=tsl):
                                    wait(sync, "vq", max(h_wr[(f2, tsl)] for f2 in range(8)))
                                    wait(sync, "ac", HRD_FREE[0])
                                    for f2 in range(8):
                                        sync.dma_start(HRD[:, 512 * f2:512 * (f2 + 1)],
                                                       h_s[tsl, f2, :, :]).then_inc(SEM["sy"], 16)
                                        CNT["sy"] += 16
                                g1 = CNT["sy"]
                                H_RD_LAST[0] = g1
                                @block.vector
                                def _(vector, tsl=tsl):
                                    tq2 = tsl % 2
                                    wait(vector, "sy", g1)
                                    vector.tensor_reduce(COLS[:, 6:7], HRD[:], AX.X, ALU.max,
                                                         apply_absolute_value=True)
                                    vector.tensor_tensor(COLS[:, 4:5], COLS[:, 8 + tsl:9 + tsl],
                                                         CST[:, 1:2], ALU.mult)
                                    vector.tensor_copy(COLS3[:], COLS[:])
                                    vector.tensor_scalar(COLS2[:, 8:9], COLS[:, 6:7], 1e-5,
                                                         None, ALU.max)
                                    vector.tensor_tensor(COLS[:, 5:6], COLS[:, 4:5],
                                                         COLS[:, 4:5], ALU.mult)
                                    vector.tensor_copy(COLS3[:], COLS[:])
                                    vector.reciprocal(COLS[:, 7:8], COLS2[:, 8:9])
                                    vector.tensor_tensor(COLS2[:, 9:10], COLS[:, 5:6],
                                                         COLS[:, 4:5], ALU.mult)
                                    vector.tensor_copy(COLS3[:], COLS[:])
                                    vector.tensor_copy(COLS3[:], COLS[:])
                                    vector.tensor_tensor(COLS2[:, 10:11], COLS2[:, 9:10],
                                                         COLS2[:, 8:9], ALU.mult)
                                    vector.tensor_copy(COLS3[:], COLS[:])
                                    vector.tensor_copy(COLS3[:], COLS[:])
                                    vector.tensor_tensor(COLS2[:, 11:12], COLS2[:, 10:11],
                                                         WSC[:, 0:1], ALU.mult)
                                    vector.tensor_copy(COLS3[:], COLS[:])
                                    vector.tensor_copy(COLS3[:], COLS[:])
                                    vector.tensor_scalar(COLS2[:, 12:13], COLS2[:, 11:12], 1e-5,
                                                         None, ALU.max)
                                    vector.tensor_copy(COLS3[:], COLS[:])
                                    vector.tensor_copy(COLS3[:], COLS[:])
                                    vector.tensor_tensor(COLS[:, 12 + tq2:13 + tq2],
                                                         COLS2[:, 12:13],
                                                         WSC[:, 1:2], ALU.mult)
                                    vector.tensor_copy(COLS3[:], COLS[:])
                                    inc("dv", vector.tensor_copy(COLS3[:], COLS[:]))
                                ckm2r = CNT["dv"]
                                @block.scalar
                                def _(scalar):
                                    wait(scalar, "dv", ckm2r)
                                    scalar.activation(COLS2[:, 2:3], COLS[:, 7:8],
                                                      AF.Identity, bias=CST[:, 5:6],
                                                      scale=127.0)
                                    scalar.copy(COLS3[:], COLS2[:])
                                    inc("ac", scalar.copy(COLS3[:], COLS2[:]))
                                for hf in range(2):
                                    @block.scalar
                                    def _(scalar, tsl=tsl, hf=hf, tq=tq):
                                        if WDB_MM[0]:
                                            wait(scalar, "pe", WDB_MM[0])
                                        scalar.activation(QF[:], HRD[:, 2048 * hf:2048 * (hf + 1)],
                                                          AF.Identity, bias=CST[:, 5:6],
                                                          scale=COLS2[:, 2:3])
                                        scalar.copy(QI[:], QF[:])
                                        inc("ac", scalar.copy(
                                            X2Q[:, 4096 * tq + 2048 * hf:4096 * tq + 2048 * (hf + 1)],
                                            QI[:]))
                                    QF_FREE[0] = CNT["ac"]
                                HRD_FREE[0] = CNT["ac"]
                                ckx2 = CNT["ac"]
                                @block.tensor
                                def _(tensor, tq=tq):
                                    wait(tensor, "ac", ckx2)
                                    if X2_EV[0]:
                                        wait(tensor, "ac", X2_EV[0])
                                    for ci in range(32):
                                        inc("pe", tensor.transpose(
                                            PSBH[:, 128 * ci:128 * (ci + 1)],
                                            X2Q[:, 4096 * tq + 128 * ci:4096 * tq + 128 * (ci + 1)],
                                            ID16[:]))
                                ckx2t = CNT["pe"]
                                @block.scalar
                                def _(scalar, tq=tq):
                                    wait(scalar, "pe", ckx2t)
                                    inc("ac", scalar.copy(X2Q[:, 4096 * tq:4096 * (tq + 1)],
                                                          PSBH[:, 0:4096]))
                                X2_EV[0] = CNT["ac"]
                                ckx2e_pair[tq] = CNT["ac"]
                            for fo in range(2):
                                for cih in range(2):
                                    g0d = CNT["sy"]
                                    @block.sync
                                    def _(sync, fo=fo, cih=cih):
                                        wait(sync, "dv", ck_attn_all)
                                        if WDB_MM[0]:
                                            wait(sync, "pe", WDB_MM[0])
                                        for c2i in range(16):
                                            ci = 16 * cih + c2i
                                            sync.dma_start(WDB[:, 512 * c2i:512 * (c2i + 1)],
                                                           wdT_s[ci, fo, :, :]).then_inc(SEM["sy"], 16)
                                            CNT["sy"] += 16
                                    g1d = CNT["sy"]
                                    @block.tensor
                                    def _(tensor, fo=fo, cih=cih, pair=pair):
                                        wait(tensor, "sy", g1d)
                                        wait(tensor, "ac", ckx2e_pair[1])
                                        if pair == 0 and fo == 0 and cih == 0:
                                            wait(tensor, "dv", DN_EV[0])
                                            wait(tensor, "dv", H_EV[0])
                                        for tq in range(2):
                                            for c2i in range(16):
                                                ci = 16 * cih + c2i
                                                mm = tensor.matmul(
                                                    PSA[:, 1024 * tq + 512 * fo:1024 * tq + 512 * fo + 512],
                                                    X2Q[:, 4096 * tq + 128 * ci:4096 * tq + 128 * (ci + 1)],
                                                    WDB[:, 512 * c2i:512 * (c2i + 1)],
                                                    start=(ci == 0), stop=(ci == 31))
                                            inc("pe", mm)
                                    WDB_MM[0] = CNT["pe"]
                            ckdn = CNT["pe"]
                            for tsl in tsls:
                                tq = tsl % 2
                                @block.scalar
                                def _(scalar, tq=tq):
                                    wait(scalar, "pe", ckdn)
                                    if DNE_FREE[0]:
                                        wait(scalar, "dv", DNE_FREE[0])
                                    if DEBUG and DNEDBG[0]:
                                        wait(scalar, "vq", DNEDBG[0])
                                    inc("ac", scalar.copy(DNE[:],
                                                          PSA[:, 1024 * tq:1024 * tq + 1024]))
                                ck_dne = CNT["ac"]
                                if DEBUG and b == 0 and tc4 == 0 and tsl == 0:
                                    @block.sync
                                    def _(sync):
                                        wait(sync, "ac", ck_dne)
                                        sync.dma_start(dbg["dne"],
                                                       DNE[:]).then_inc(SEM["vq"], 16)
                                        CNT["vq"] += 16
                                    DNEDBG[0] = CNT["vq"]
                                @block.vector
                                def _(vector, tsl=tsl, tq=tq):
                                    wait(vector, "ac", ck_dne)
                                    vector.scalar_tensor_tensor(RES[:], DNE[:],
                                                                COLS[:, 12 + tq:13 + tq],
                                                                Y1B[:, 1024 * tsl:1024 * (tsl + 1)],
                                                                ALU.mult, ALU.add)
                                    vector.tensor_reduce(COLS[:, 0:1], RES[:], AX.X, ALU.add)
                                    vector.tensor_copy(COLS3[:], COLS[:])
                                    inc("dv", vector.tensor_copy(COLS3[:], COLS[:]))
                                ckc2 = CNT["dv"]
                                DNE_FREE[0] = ckc2
                                @block.scalar
                                def _(scalar):
                                    wait(scalar, "dv", ckc2)
                                    scalar.activation(COLS2[:, 1:2], COLS[:, 0:1],
                                                      AF.Identity, bias=CST[:, 5:6],
                                                      scale=-1.0 / 1024.0)
                                    scalar.copy(COLS3[:], COLS2[:])
                                    scalar.copy(COLS3[:], COLS2[:])
                                    scalar.activation(CEN[:], RES[:], AF.Identity,
                                                      bias=COLS2[:, 1:2], scale=1.0)
                                    scalar.activation(RES[:], CEN[:], AF.Square,
                                                      bias=CST[:, 5:6], scale=1.0,
                                                      accum_out=COLS[:, 1:2])
                                    inc("ac", scalar.copy(COLS3[:], COLS[:]))
                                ckv2 = CNT["ac"]
                                @block.vector
                                def _(vector):
                                    wait(vector, "ac", ckv2)
                                    vector.tensor_scalar(COLS2[:, 6:7], COLS[:, 1:2], 1.0 / 1024.0,
                                                         1e-5, ALU.mult, ALU.add)
                                    vector.tensor_copy(COLS3[:], COLS[:])
                                    vector.tensor_copy(COLS3[:], COLS[:])
                                    vector.reciprocal(COLS[:, 2:3], COLS2[:, 6:7])
                                    vector.tensor_copy(COLS3[:], COLS[:])
                                    inc("dv", vector.tensor_copy(COLS3[:], COLS[:]))
                                ckr2 = CNT["dv"]
                                DN_EV[0] = CNT["dv"]
                                @block.scalar
                                def _(scalar, tsl=tsl, tc4=tc4):
                                    wait(scalar, "dv", ckr2)
                                    if OUT_ST[0]:
                                        wait(scalar, "vq", OUT_ST[0])
                                    scalar.sqrt(COLS[:, 3:4], COLS[:, 2:3])
                                    scalar.copy(COLS3[:], COLS[:])
                                    scalar.copy(COLS3[:], COLS[:])
                                    scalar.activation(OUTB[:], CEN[:], AF.Identity,
                                                      bias=CST[:, 5:6],
                                                      scale=COLS[:, 3:4])
                                    scalar.copy(COLS3[:], OUTB[:, 1008:1024])
                                    inc("ac", scalar.copy(COLS3[:], OUTB[:, 1008:1024]))
                                ck_outb = CNT["ac"]
                                @block.sync
                                def _(sync, tsl=tsl, tc4=tc4):
                                    wait(sync, "ac", ck_outb)
                                    ts_g = 4 * tc4 + tsl
                                    srcs = {"out": OUTB.ap()[:], "prjb": PRJB.ap()[:],
                                            "xpb": XPB.ap()[:], "cen": CEN.ap()[:],
                                            "y1": Y1B.ap()[:, 1024 * tsl:1024 * (tsl + 1)],
                                            "qf": QF.ap()[:, 0:1024]}
                                    sync.dma_start(out_d[b, 128 * ts_g:128 * (ts_g + 1), :],
                                                   srcs[TAP]).then_inc(SEM["vq"], 16)
                                    CNT["vq"] += 16
                                OUT_ST[0] = CNT["vq"]
                    BATCH_GATE[0] = CNT["dv"]
                    OUT_ST_B[0] = CNT["vq"]

        if DEBUG:
            @block.sync
            def _(sync):
                wait(sync, "vq", CNT["vq"])
                for ci in range(8):
                    sync.dma_start(dbg["wgb"][:, 512 * ci:512 * (ci + 1)],
                                   wgT_s[ci, 0, :, :]).then_inc(SEM["vq"], 16)
                    CNT["vq"] += 16
                sync.dma_start(dbg["wgall"], wgT_s[:, :, :, :]).then_inc(SEM["vq"], 16)
                sync.dma_start(dbg["wuall"], wuT_s[:, :, :, :]).then_inc(SEM["vq"], 16)
                sync.dma_start(dbg["wdall"], wdT_s[:, :, :, :]).then_inc(SEM["vq"], 16)
                sync.dma_start(dbg["scl"], SCL[:]).then_inc(SEM["vq"], 16)
                CNT["vq"] += 64
        ck_allst = CNT["vq"]
        @block.gpsimd
        def _(gpsimd):
            wait(gpsimd, "vq", ck_allst)

    return nc


def kernel(**kw):
    ren = {"L0_attn_x": "x", "L0_attn_qkv_w": "qkv_w", "L0_attn_qkv_b": "qkv_b",
           "L0_attn_proj_w": "proj_w", "L0_attn_proj_b": "proj_b",
           "L0_attn_dwc_w": "dwc_w", "L0_attn_dwc_b": "dwc_b",
           "L0_ffn_w_gate": "w_gate", "L0_ffn_w_up": "w_up", "L0_ffn_w_down": "w_down"}
    inputs = {ren.get(k, k): v for k, v in kw.items()}
    xfull = np.ascontiguousarray(np.asarray(inputs["x"], dtype=np.float32))
    w = {k: np.ascontiguousarray(np.asarray(inputs[k], dtype=np.float32))
         for k in ("qkv_w", "qkv_b", "proj_w", "proj_b", "dwc_b",
                   "w_gate", "w_up", "w_down")}
    w["dwc_w"] = np.ascontiguousarray(
        np.asarray(inputs["dwc_w"], dtype=np.float32).reshape(C, 3))

    ident = np.eye(128, dtype=np.float16)
    masks = np.zeros((128, 128 * 64), dtype=np.float16)
    for t in range(64):
        masks[0:64, 128 * t + 2 * t] = 1.0
        masks[64:128, 128 * t + 2 * t + 1] = 1.0

    qkb_pre = np.ascontiguousarray(w["qkv_b"].reshape(24, 128).T)
    dwb_pre = np.ascontiguousarray(w["dwc_b"].reshape(8, 128).T)
    dww_pre = np.ascontiguousarray(
        w["dwc_w"].reshape(8, 128, 3).transpose(1, 0, 2).reshape(128, 24))
    prjb_pre = np.ascontiguousarray(np.broadcast_to(w["proj_b"], (128, 1024)))

    nc = build_nc()
    in_maps = []
    for c in range(8):
        m = {"x": xfull[BPC * c:BPC * (c + 1)], "identity16": ident, "masks16": masks,
             "qkb_pre": qkb_pre, "dwb_pre": dwb_pre, "dww_pre": dww_pre,
             "prjb_pre": prjb_pre}
        m.update(w)
        in_maps.append(m)
    res = run_bass_kernel_spmd(nc, in_maps, core_ids=list(range(8)))
    return np.concatenate([r["out"] for r in res.results], axis=0)
